# version 02 (final)
# baseline (speedup 1.0000x reference)
import os
import numpy as np
import ml_dtypes
import concourse.bass as bass
import concourse.mybir as mybir
from concourse.bass_utils import run_bass_kernel_spmd

F32 = mybir.dt.float32
BF16 = mybir.dt.bfloat16
AF = mybir.ActivationFunctionType
ALU = mybir.AluOpType
AX = mybir.AxisListType
PI = float(np.pi)

D_MODEL = 1024
D_INNER = 2048
D_FF = 4096
N_STATE = 16
NCG = 16
NDC = 8
NFC = 32
TT = 512
EPS = 1e-5

N_CORES = 8



class Builder:

    ENG = {"pe": "tensor", "act": "scalar", "dve": "vector", "pool": "gpsimd",
           "sync": "sync"}

    DMA_CH = ("qw0", "qw1", "qx", "qm")

    def __init__(self):
        self.count = {c: 0 for c in
                      ("pe", "act", "dve", "pool") + self.DMA_CH}
        self.chan_engine = {"pe": "pe", "act": "act", "dve": "dve",
                            "pool": "pool"}
        for c in self.DMA_CH:
            self.chan_engine[c] = "sync"
        self.max_waited = {c: 0 for c in self.DMA_CH}
        self.bufs = {}
        self.floor = {e: {} for e in ("pe", "act", "dve", "pool", "sync")}
        self.prog = {e: [] for e in ("pe", "act", "dve", "pool", "sync")}
        self.sems = {}

    def _need(self, names_r, names_w):
        need = {}
        for nm in names_r:
            b = self.bufs.get(nm)
            if b and b["w"]:
                ch, cnt = b["w"]
                need[ch] = max(need.get(ch, 0), cnt)
        for nm in names_w:
            b = self.bufs.get(nm)
            if b:
                if b["w"]:
                    ch, cnt = b["w"]
                    need[ch] = max(need.get(ch, 0), cnt)
                for ch, cnt in b["r"].items():
                    need[ch] = max(need.get(ch, 0), cnt)
        return need

    @staticmethod
    def _names(aps):
        out = []
        for a in aps:
            if a is None:
                continue
            if isinstance(a, str):
                out.append(a)
            else:
                out.append(a.tensor.name)
        return out

    @staticmethod
    def _split_psum(names_r, names_w):
        keep_r = [n for n in names_r if not n.startswith("ps")]
        mv = [n for n in names_r if n.startswith("ps")]
        return keep_r, names_w + mv

    def emit(self, chan, fn, reads=(), writes=(), n_inc=1):
        eng = self.chan_engine[chan]
        names_r = self._names(reads)
        names_w = self._names(writes)
        names_r, names_w = self._split_psum(names_r, names_w)
        need = self._need(names_r, names_w)
        if chan in self.DMA_CH:
            need[chan] = max(need.get(chan, 0), self.max_waited[chan])
        fl = self.floor[eng]
        waits = []
        for ch, cnt in sorted(need.items()):
            if fl.get(ch, 0) >= cnt:
                continue
            if ch in self.DMA_CH:
                cnt = self.count[ch]
                self.max_waited[ch] = max(self.max_waited[ch], cnt)
            if fl.get(ch, 0) >= cnt:
                continue
            waits.append((ch, cnt))
            fl[ch] = cnt
        inc_amt = 16 if chan in self.DMA_CH else n_inc
        self.count[chan] += inc_amt
        cnt_after = self.count[chan]
        sems = self.sems

        def closure(eng_obj, waits=waits, chan=chan, inc_amt=inc_amt, fn=fn):
            for ch, cnt in waits:
                eng_obj.wait_ge(sems[ch], cnt)
            ins = fn(eng_obj)
            ins.then_inc(sems[chan], inc_amt)

        self.prog[eng].append(closure)
        for nm in names_w:
            self.bufs[nm] = {"w": (chan, cnt_after), "r": {}}
        for nm in names_r:
            b = self.bufs.setdefault(nm, {"w": None, "r": {}})
            b["r"][chan] = cnt_after
        return cnt_after

    def emit_group(self, chan, fns, reads=(), writes=()):
        eng = self.chan_engine[chan]
        names_r = self._names(reads)
        names_w = self._names(writes)
        names_r, names_w = self._split_psum(names_r, names_w)
        need = self._need(names_r, names_w)
        fl = self.floor[eng]
        waits = []
        for ch, cnt in sorted(need.items()):
            if fl.get(ch, 0) >= cnt:
                continue
            if ch in self.DMA_CH:
                cnt = self.count[ch]
                self.max_waited[ch] = max(self.max_waited[ch], cnt)
            if fl.get(ch, 0) >= cnt:
                continue
            waits.append((ch, cnt))
            fl[ch] = cnt
        self.count[chan] += 1
        cnt_after = self.count[chan]
        sems = self.sems

        def closure(eng_obj, waits=waits, fns=list(fns), chan=chan):
            for ch, cnt in waits:
                eng_obj.wait_ge(sems[ch], cnt)
            last = None
            for f in fns:
                last = f(eng_obj)
            last.then_inc(sems[chan], 1)

        self.prog[eng].append(closure)
        for nm in names_w:
            self.bufs[nm] = {"w": (chan, cnt_after), "r": {}}
        for nm in names_r:
            b = self.bufs.setdefault(nm, {"w": None, "r": {}})
            b["r"][chan] = cnt_after
        return cnt_after

    def barrier_read(self, chan, names):
        return self._need(self._names(names), [])



def build_program(n_tiles):
    nc = bass.Bass("TRN2", target_bir_lowering=False)
    T_core = n_tiles * TT

    xT = nc.dram_tensor("xT", [D_MODEL, T_core], F32, kind="ExternalInput").ap()
    yT = nc.dram_tensor("yT", [D_MODEL, T_core], F32, kind="ExternalOutput").ap()
    dbg = nc.dram_tensor("dbg", [128, 2 * NCG * (TT + 3)], BF16, kind="ExternalOutput").ap()
    KDUMP = os.environ.get("KDUMP", "")
    b1r = nc.dram_tensor("b1r", [1, D_FF], F32, kind="ExternalInput").ap()
    b2r = nc.dram_tensor("b2r", [1, D_MODEL], F32, kind="ExternalInput").ap()
    bdtr = nc.dram_tensor("bdtr", [1, D_INNER], F32, kind="ExternalInput").ap()
    thc = nc.dram_tensor("thc", [128, NFC], F32, kind="ExternalInput").ap()
    wrc = nc.dram_tensor("wrc", [128, NFC], F32, kind="ExternalInput").ap()
    wic = nc.dram_tensor("wic", [128, NFC], F32, kind="ExternalInput").ap()
    Alog = nc.dram_tensor("Alog", [D_INNER, N_STATE], F32, kind="ExternalInput").ap()
    Dc = nc.dram_tensor("Dc", [128, NCG], F32, kind="ExternalInput").ap()
    bconvc = nc.dram_tensor("bconvc", [128, NCG], F32, kind="ExternalInput").ap()
    wconvc = nc.dram_tensor("wconvc", [D_INNER, 4], F32, kind="ExternalInput").ap()

    w1sl = nc.dram_tensor("w1sl", [NFC, 128, NDC * 128], F32, kind="ExternalInput").ap()
    w2sl = nc.dram_tensor("w2sl", [NDC, 128, NFC * 128], F32, kind="ExternalInput").ap()
    winsl = nc.dram_tensor("winsl", [2 * NCG, 128, NDC * 128], F32, kind="ExternalInput").ap()
    wdtsl = nc.dram_tensor("wdtsl", [NCG, 128, NCG * 128], F32, kind="ExternalInput").ap()
    woutsl = nc.dram_tensor("woutsl", [NDC, 128, NCG * 128], F32, kind="ExternalInput").ap()
    wxsl = nc.dram_tensor("wxsl", [128, NCG * 128], F32, kind="ExternalInput").ap()
    wdts = nc.dram_tensor("wdts", [NCG, 128, NCG * 128], BF16, kind="Internal").ap()
    w1s = nc.dram_tensor("w1s", [NFC, 128, NDC * 128], BF16, kind="Internal").ap()
    w2s = nc.dram_tensor("w2s", [NDC, 128, NFC * 128], BF16, kind="Internal").ap()
    wins = nc.dram_tensor("wins", [2 * NCG, 128, NDC * 128], BF16, kind="Internal").ap()
    wouts = nc.dram_tensor("wouts", [NDC, 128, NCG * 128], BF16, kind="Internal").ap()
    wxs = nc.dram_tensor("wxs", [128, NCG * 128], BF16, kind="Internal").ap()
    bstg = nc.dram_tensor("bstg", [2, N_STATE * TT], BF16, kind="Internal").ap()

    import contextlib
    bd = Builder()
    with contextlib.ExitStack() as ctx:
        def sb(name, shape, dtype):
            return ctx.enter_context(nc.sbuf_tensor(name, shape, dtype)).ap()

        xb0 = sb("xb0", [128, NDC * TT], F32)
        xb1 = sb("xb1", [128, NDC * TT], F32)
        state = sb("state", [128, NCG * N_STATE], F32)
        carry = sb("carry", [128, NCG * 3], BF16)
        A4 = sb("A4", [128, NCG * N_STATE], F32)
        Dcol = sb("Dcol", [128, NCG], F32)
        bcv = sb("bcv", [128, NCG], F32)
        wcv = sb("wcv", [128, NCG * 4], F32)
        cs1 = sb("cs1", [128, NFC], F32)
        cs2 = sb("cs2", [128, NFC], F32)
        b1b = sb("b1b", [1, D_FF], BF16)
        b2b = sb("b2b", [1, D_MODEL], BF16)
        bdtb = sb("bdtb", [1, D_INNER], BF16)
        ones_row = sb("ones_row", [1, TT], BF16)
        ones_col = sb("ones_col", [128, 1], BF16)
        ones_128 = sb("ones_128", [1, 128], BF16)

        const_tiles = {}

        def cst(val):
            if val not in const_tiles:
                ap = sb(f"cst{len(const_tiles)}", [128, 1], F32)
                E("dve", lambda v, ap=ap, val=val: v.memset(ap, float(val)),
                  writes=[ap])
                nc.const_aps.register(F32, float(val), ap)
                const_tiles[val] = ap
            return const_tiles[val]

        xbb = [sb(f"xbb{i}", [128, TT], BF16) for i in range(2)]
        xsq = [sb(f"xsq{i}", [128, TT], BF16) for i in range(2)]
        xn = sb("xn", [128, NDC * TT], BF16)
        mrow = sb("mrow", [1, TT], F32)
        qrow = sb("qrow", [1, TT], F32)
        vrow = sb("vrow", [1, TT], F32)
        mrow_b = sb("mrow_b", [1, TT], BF16)
        rrow_b = sb("rrow_b", [1, TT], BF16)
        magicrow = sb("magicrow", [1, TT], F32)
        nrow = sb("nrow", [1, TT], F32)

        raw = sb("raw", [128, NCG * (TT + 3)], BF16)
        gbuf = sb("gbuf", [128, NCG * TT], BF16)
        xs = sb("xs", [128, NCG * TT], BF16)
        brep = sb("brep", [128, N_STATE * TT], BF16)
        bbig = sb("bbig", [128, N_STATE * TT], BF16)
        hbuf = sb("hbuf", [128, N_STATE * TT], BF16)
        tr = sb("tr", [128, 8 * TT], BF16)
        dtb = [sb(f"dtb{i}", [128, TT], F32) for i in range(2)]
        ub = [sb(f"ub{i}", [128, TT], BF16) for i in range(2)]
        yb = [sb(f"yb{i}", [128, TT], F32) for i in range(2)]
        sf = [sb(f"sf{i}", [128, TT], F32) for i in range(2)]
        Bsb = sb("Bsb", [N_STATE, TT], BF16)
        winslab = [sb(f"winslab{i}", [128, NDC * 128], BF16) for i in range(2)]
        wdtslab = [sb(f"wdtslab{i}", [128, NCG * 128], BF16) for i in range(2)]


        def act_slice(f):
            t = bbig if f < 16 else hbuf
            return t[:, (f % 16) * TT:(f % 16 + 1) * TT]
        sv = [brep[:, i * TT:(i + 1) * TT] for i in range(2)]
        cv = [brep[:, (2 + i) * TT:(3 + i) * TT] for i in range(2)]
        a1 = [brep[:, (4 + i) * TT:(5 + i) * TT] for i in range(2)]
        t2 = [brep[:, (6 + i) * TT:(7 + i) * TT] for i in range(2)]
        w1slab = [tr[:, i * 1024:(i + 1) * 1024] for i in range(2)]
        w2slab = [xs[:, i * 4096:(i + 1) * 4096] for i in range(2)]
        wtmp_f = gbuf.bitcast(F32)
        wtmp_b = hbuf

        ps = [ctx.enter_context(nc.psum_tensor(f"ps{i}", [128, TT], F32)).ap()
              for i in range(8)]

        for ch in ("pe", "act", "dve", "pool", "qw0", "qw1", "qx", "qm"):
            bd.sems[ch] = ctx.enter_context(nc.semaphore("sem_" + ch))

        E = bd.emit
        G = bd.emit_group

        E("dve", lambda v: v.memset(state, 0.0), writes=[state])
        E("dve", lambda v: v.memset(carry, 0.0), writes=[carry])
        E("dve", lambda v: v.memset(ones_row, 1.0), writes=[ones_row])
        E("dve", lambda v: v.memset(ones_col, 1.0), writes=[ones_col])
        E("dve", lambda v: v.memset(ones_128, 1.0), writes=[ones_128])
        for _v in (0.1, PI / 2, -1.0, 1.0, 0.5, -0.5, EPS, 2 * PI, -2 * PI,
                   -PI, PI, 0.0, -2.0, 2.0, 1.0 / D_MODEL):
            cst(_v)
        I32 = mybir.dt.int32
        E("dve", lambda v: v.memset(magicrow.bitcast(I32), 0x5f3759df),
          writes=[magicrow])

        def stage_weight(srcsl, dst, noc, w):
            for o in range(noc):
                qch = "qw" + str(o % 2)
                fview = wtmp_f[:, 0:w]
                E(qch, lambda s, o=o, fv=fview, srcsl=srcsl: s.dma_start(
                    out=fv, in_=srcsl[o]),
                  reads=[], writes=[wtmp_f.tensor.name])
                bview = wtmp_b[:, 0:w]
                E("dve", lambda v, fv=fview, bv=bview: v.tensor_copy(out=bv, in_=fv),
                  reads=[wtmp_f.tensor.name], writes=[wtmp_b.tensor.name])
                E(qch, lambda s, o=o, bv=bview, dst=dst: s.dma_start(
                    out=dst[o], in_=bv),
                  reads=[wtmp_b.tensor.name], writes=[dst.tensor.name])

        stage_weight(w1sl, w1s, NFC, NDC * 128)
        stage_weight(w2sl, w2s, NDC, NFC * 128)
        stage_weight(winsl, wins, 2 * NCG, NDC * 128)
        stage_weight(wdtsl, wdts, NCG, NCG * 128)
        stage_weight(woutsl, wouts, NDC, NCG * 128)
        fview = wtmp_f[:, 0:NCG * 128]
        E("qw0", lambda s, fv=fview: s.dma_start(out=fv, in_=wxsl),
          writes=[wtmp_f.tensor.name])
        bview = wtmp_b[:, 0:NCG * 128]
        E("dve", lambda v, fv=fview, bv=bview: v.tensor_copy(out=bv, in_=fv),
          reads=[wtmp_f.tensor.name], writes=[wtmp_b.tensor.name])
        E("qw0", lambda s, bv=bview: s.dma_start(out=wxs, in_=bv),
          reads=[wtmp_b.tensor.name], writes=[wxs.tensor.name])

        for cg in range(NCG):
            E("qm", lambda s, cg=cg: s.dma_start(
                out=A4[:, cg * N_STATE:(cg + 1) * N_STATE],
                in_=Alog[cg * 128:(cg + 1) * 128, :]), writes=[A4])
        E("act", lambda a: a.activation(out=A4, in_=A4, func=AF.Exp),
          reads=[A4], writes=[A4])
        E("dve", lambda v: v.tensor_single_scalar(out=A4, in_=A4, scalar=-1.0,
                                                  op=ALU.mult),
          reads=[A4], writes=[A4])
        E("qm", lambda s: s.dma_start(out=Dcol, in_=Dc), writes=[Dcol])
        E("qm", lambda s: s.dma_start(out=bcv, in_=bconvc), writes=[bcv])
        for cg in range(NCG):
            E("qm", lambda s, cg=cg: s.dma_start(
                out=wcv[:, cg * 4:(cg + 1) * 4],
                in_=wconvc[cg * 128:(cg + 1) * 128, :]), writes=[wcv])
        def stage_row(src_row, dst_b, n):
            fview = wtmp_f[0:1, 0:n]
            E("qm", lambda s, fv=fview, sr=src_row: s.dma_start(out=fv, in_=sr),
              writes=[wtmp_f.tensor.name])
            E("dve", lambda v, fv=fview, db=dst_b: v.tensor_copy(out=db, in_=fv),
              reads=[wtmp_f.tensor.name], writes=[dst_b])

        stage_row(b1r, b1b, D_FF)
        stage_row(b2r, b2b, D_MODEL)
        stage_row(bdtr, bdtb, D_INNER)

        th = sb("th", [128, NFC], F32)
        wr_ = sb("wr_", [128, NFC], F32)
        wi_ = sb("wi_", [128, NFC], F32)
        tmpc = sb("tmpc", [128, NFC], F32)
        tmpc2 = sb("tmpc2", [128, NFC], F32)
        E("qm", lambda s: s.dma_start(out=th, in_=thc), writes=[th])
        E("qm", lambda s: s.dma_start(out=wr_, in_=wrc), writes=[wr_])
        E("qm", lambda s: s.dma_start(out=wi_, in_=wic), writes=[wi_])
        for _ in range(6):
            E("dve", lambda v: v.tensor_single_scalar(out=tmpc, in_=th,
                                                      scalar=PI, op=ALU.is_gt),
              reads=[th], writes=[tmpc])
            E("dve", lambda v: v.scalar_tensor_tensor(out=th, in0=tmpc,
                                                      scalar=-2 * PI, in1=th,
                                                      op0=ALU.mult, op1=ALU.add),
              reads=[tmpc, th], writes=[th])
            E("dve", lambda v: v.tensor_single_scalar(out=tmpc, in_=th,
                                                      scalar=-PI, op=ALU.is_lt),
              reads=[th], writes=[tmpc])
            E("dve", lambda v: v.scalar_tensor_tensor(out=th, in0=tmpc,
                                                      scalar=2 * PI, in1=th,
                                                      op0=ALU.mult, op1=ALU.add),
              reads=[tmpc, th], writes=[th])
        sth = sb("sth", [128, NFC], F32)
        cth = sb("cth", [128, NFC], F32)
        E("act", lambda a: a.activation(out=sth, in_=th, func=AF.Sin),
          reads=[th], writes=[sth])
        E("act", lambda a: a.activation(out=tmpc, in_=th, func=AF.Sin, scale=0.5),
          reads=[th], writes=[tmpc])
        E("dve", lambda v: v.tensor_tensor(out=tmpc2, in0=tmpc, in1=tmpc,
                                           op=ALU.mult),
          reads=[tmpc], writes=[tmpc2])
        E("dve", lambda v: v.tensor_scalar(out=cth, in0=tmpc2, scalar1=-2.0,
                                           scalar2=1.0, op0=ALU.mult, op1=ALU.add),
          reads=[tmpc2], writes=[cth])
        E("dve", lambda v: v.tensor_tensor(out=tmpc, in0=cth, in1=wr_, op=ALU.mult),
          reads=[cth, wr_], writes=[tmpc])
        E("dve", lambda v: v.tensor_tensor(out=tmpc2, in0=sth, in1=wi_, op=ALU.mult),
          reads=[sth, wi_], writes=[tmpc2])
        E("dve", lambda v: v.tensor_tensor(out=tmpc, in0=tmpc, in1=tmpc2, op=ALU.add),
          reads=[tmpc, tmpc2], writes=[tmpc])
        E("dve", lambda v: v.tensor_tensor(out=tmpc, in0=tmpc, in1=wi_, op=ALU.mult),
          reads=[tmpc, wi_], writes=[tmpc])
        E("dve", lambda v: v.tensor_single_scalar(out=tmpc, in_=tmpc, scalar=0.1,
                                                  op=ALU.mult),
          reads=[tmpc], writes=[tmpc])
        E("dve", lambda v: v.tensor_tensor(out=cs1, in0=tmpc, in1=sth, op=ALU.mult),
          reads=[tmpc, sth], writes=[cs1])
        E("dve", lambda v: v.tensor_tensor(out=cs2, in0=tmpc, in1=cth, op=ALU.mult),
          reads=[tmpc, cth], writes=[cs2])

        STAGES = set(os.environ.get(
            "KSTAGES",
            "ffn1,ln2,win,conv,bx,dt,dtsp,dtu,dtb,scan,gate,wout,ffn2").split(","))
        xbufs = [xb0, xb1]
        dumped = [False]

        def dump(tag, ap_list):
            if KDUMP != tag or dumped[0]:
                return
            dumped[0] = True
            off = 0
            for apv in ap_list:
                w = int(np.prod(apv.shape[1:]))
                if apv.dtype == F32:
                    E("dve", lambda v, apv=apv, w=w: v.tensor_copy(
                        out=tr[:, 0:w], in_=apv),
                      reads=[apv], writes=[tr])
                    apv = tr[:, 0:w]
                dv_ = dbg[:, off:off + w]
                off += w
                E("qm", lambda s, apv=apv, dv_=dv_: s.dma_start(
                    out=dv_, in_=apv),
                  reads=[apv], writes=[dbg.tensor.name])

        def xch(par, dc):
            return xbufs[par][:, dc * TT:(dc + 1) * TT]

        def emit_ln(par):
            xv = xbufs[par]
            mm_sx = []
            mm_sq = []
            for dc in range(NDC):
                E("dve", lambda v, dc=dc, par=par: v.tensor_copy(
                    out=xbb[dc % 2], in_=xch(par, dc)),
                  reads=[xv], writes=[xbb[dc % 2]])
                E("dve", lambda v, dc=dc: v.tensor_tensor(
                    out=xsq[dc % 2], in0=xbb[dc % 2], in1=xbb[dc % 2],
                    op=ALU.mult),
                  reads=[xbb[dc % 2]], writes=[xsq[dc % 2]])
                G("pe", [lambda t, dc=dc: t.matmul(
                    out=ps[0][0:1, :], lhsT=ones_col, rhs=xbb[dc % 2],
                    start=(dc == 0), stop=(dc == NDC - 1))],
                  reads=[ones_col, xbb[dc % 2]], writes=[ps[0]])
                G("pe", [lambda t, dc=dc: t.matmul(
                    out=ps[1][0:1, :], lhsT=ones_col, rhs=xsq[dc % 2],
                    start=(dc == 0), stop=(dc == NDC - 1))],
                  reads=[ones_col, xsq[dc % 2]], writes=[ps[1]])
            E("dve", lambda v: v.tensor_single_scalar(
                out=mrow, in_=ps[0][0:1, :], scalar=1.0 / D_MODEL, op=ALU.mult),
              reads=[ps[0]], writes=[mrow])
            E("dve", lambda v: v.tensor_single_scalar(
                out=qrow, in_=ps[1][0:1, :], scalar=1.0 / D_MODEL, op=ALU.mult),
              reads=[ps[1]], writes=[qrow])
            E("dve", lambda v: v.tensor_tensor(out=vrow, in0=mrow, in1=mrow,
                                               op=ALU.mult),
              reads=[mrow], writes=[vrow])
            E("dve", lambda v: v.tensor_tensor(out=vrow, in0=qrow, in1=vrow,
                                               op=ALU.subtract),
              reads=[qrow, vrow], writes=[vrow])
            E("dve", lambda v: v.tensor_single_scalar(out=vrow, in0=vrow,
                                                      scalar=EPS, op=ALU.add)
              if False else v.tensor_scalar(out=vrow, in0=vrow, scalar1=EPS,
                                            scalar2=None, op0=ALU.add),
              reads=[vrow], writes=[vrow])
            I32 = mybir.dt.int32
            E("dve", lambda v: v.tensor_scalar(
                out=qrow.bitcast(I32), in0=vrow.bitcast(I32), scalar1=1,
                scalar2=None, op0=ALU.logical_shift_right),
              reads=[vrow], writes=[qrow])
            E("dve", lambda v: v.tensor_tensor(
                out=qrow.bitcast(I32), in0=magicrow.bitcast(I32),
                in1=qrow.bitcast(I32), op=ALU.subtract),
              reads=[magicrow, qrow], writes=[qrow])
            for _it in range(2):
                E("dve", lambda v: v.tensor_tensor(out=nrow, in0=qrow, in1=qrow,
                                                   op=ALU.mult),
                  reads=[qrow], writes=[nrow])
                E("dve", lambda v: v.tensor_tensor(out=nrow, in0=nrow, in1=vrow,
                                                   op=ALU.mult),
                  reads=[nrow, vrow], writes=[nrow])
                E("dve", lambda v: v.tensor_scalar(out=nrow, in0=nrow,
                                                   scalar1=-0.5, scalar2=1.5,
                                                   op0=ALU.mult, op1=ALU.add),
                  reads=[nrow], writes=[nrow])
                E("dve", lambda v: v.tensor_tensor(out=qrow, in0=qrow, in1=nrow,
                                                   op=ALU.mult),
                  reads=[qrow, nrow], writes=[qrow])
            E("dve", lambda v: v.tensor_copy(out=mrow_b, in_=mrow),
              reads=[mrow], writes=[mrow_b])
            E("dve", lambda v: v.tensor_copy(out=rrow_b, in_=qrow),
              reads=[qrow], writes=[rrow_b])
            G("pe", [lambda t: t.matmul(out=ps[2], lhsT=ones_128, rhs=mrow_b,
                                        start=True, stop=True)],
              reads=[ones_128, mrow_b], writes=[ps[2]])
            G("pe", [lambda t: t.matmul(out=ps[3], lhsT=ones_128, rhs=rrow_b,
                                        start=True, stop=True)],
              reads=[ones_128, rrow_b], writes=[ps[3]])
            for dc in range(NDC):
                xnv = xn[:, dc * TT:(dc + 1) * TT]
                E("dve", lambda v, dc=dc, par=par, xnv=xnv: v.tensor_tensor(
                    out=xnv, in0=xch(par, dc), in1=ps[2], op=ALU.subtract),
                  reads=[xv, ps[2]], writes=[xn])
                E("dve", lambda v, xnv=xnv: v.tensor_tensor(
                    out=xnv, in0=xnv, in1=ps[3], op=ALU.mult),
                  reads=[xn, ps[3]], writes=[xn])

        def emit_ffn(par):
            emit_ln(par)
            for f in range(NFC):
                slab = w1slab[f % 2]
                E("qw" + str(f % 2), lambda s, f=f, slab=slab: s.dma_start(
                    out=slab, in_=w1s[f]),
                  reads=[w1s.tensor.name], writes=[slab])
                bank = ps[4 + (f % 2)]
                mms = []
                for k in range(NDC):
                    mms.append(lambda t, f=f, k=k, slab=slab, bank=bank:
                               t.matmul(out=bank,
                                        lhsT=slab[:, k * 128:(k + 1) * 128],
                                        rhs=xn[:, k * TT:(k + 1) * TT],
                                        start=(k == 0), stop=False))
                mms.append(lambda t, f=f, bank=bank: t.matmul(
                    out=bank, lhsT=b1b[:, f * 128:(f + 1) * 128], rhs=ones_row,
                    start=False, stop=True))
                G("pe", mms, reads=[slab, xn, b1b, ones_row], writes=[bank])
                E("act", lambda a, f=f, bank=bank: a.activation(
                    out=sv[f % 2], in_=bank, func=AF.Sin, scale=0.1),
                  reads=[bank], writes=[sv[f % 2]])
                E("act", lambda a, f=f, bank=bank: a.activation(
                    out=cv[f % 2], in_=bank, func=AF.Sin, scale=0.1, bias=PI / 2),
                  reads=[bank], writes=[cv[f % 2]])
                E("dve", lambda v, f=f: v.tensor_single_scalar(
                    out=a1[f % 2], in_=cv[f % 2], scalar=cs1[:, f:f + 1],
                    op=ALU.mult),
                  reads=[cv[f % 2], cs1], writes=[a1[f % 2]])
                E("dve", lambda v, f=f: v.scalar_tensor_tensor(
                    out=t2[f % 2], in0=sv[f % 2], scalar=cs2[:, f:f + 1],
                    in1=a1[f % 2], op0=ALU.mult, op1=ALU.add),
                  reads=[sv[f % 2], cs2, a1[f % 2]], writes=[t2[f % 2]])
                E("dve", lambda v, f=f, bank=bank: v.scalar_tensor_tensor(
                    out=a1[f % 2], in0=t2[f % 2], scalar=1.0, in1=bank,
                    op0=ALU.mult, op1=ALU.add),
                  reads=[t2[f % 2], bank], writes=[a1[f % 2]])
                E("dve", lambda v, f=f: v.tensor_single_scalar(
                    out=act_slice(f), in_=a1[f % 2], scalar=0.0, op=ALU.max),
                  reads=[a1[f % 2]], writes=[act_slice(f)])
            for d in range(NDC):
                slab = w2slab[d % 2]
                E("qw" + str(d % 2), lambda s, d=d, slab=slab: s.dma_start(
                    out=slab, in_=w2s[d]),
                  reads=[w2s.tensor.name], writes=[slab])
                bank = ps[6 + (d % 2)]
                mms = []
                for k in range(NFC):
                    mms.append(lambda t, d=d, k=k, slab=slab, bank=bank:
                               t.matmul(out=bank,
                                        lhsT=slab[:, k * 128:(k + 1) * 128],
                                        rhs=act_slice(k),
                                        start=(k == 0), stop=False))
                mms.append(lambda t, d=d, bank=bank: t.matmul(
                    out=bank, lhsT=b2b[:, d * 128:(d + 1) * 128], rhs=ones_row,
                    start=False, stop=True))
                G("pe", mms, reads=[slab, bbig, hbuf, b2b, ones_row], writes=[bank])
                E("dve", lambda v, d=d, par=par, bank=bank: v.scalar_tensor_tensor(
                    out=xch(par, d), in0=bank, scalar=0.5, in1=xch(par, d),
                    op0=ALU.mult, op1=ALU.add),
                  reads=[bank, xbufs[par]], writes=[xbufs[par]])

        def emit_mixer(par, tile):
            if "ln2" not in STAGES:
                return
            emit_ln(par)
            if "win" not in STAGES:
                return
            for o in range(2 * NCG):
                slab = winslab[o % 2]
                E("qw" + str(o % 2), lambda s, o=o, slab=slab: s.dma_start(
                    out=slab, in_=wins[o]),
                  reads=[wins.tensor.name], writes=[slab])
                bank = ps[4 + (o % 2)]
                mms = []
                for k in range(NDC):
                    mms.append(lambda t, o=o, k=k, slab=slab, bank=bank:
                               t.matmul(out=bank,
                                        lhsT=slab[:, k * 128:(k + 1) * 128],
                                        rhs=xn[:, k * TT:(k + 1) * TT],
                                        start=(k == 0), stop=(k == NDC - 1)))
                G("pe", mms, reads=[slab, xn], writes=[bank])
                if o == 0 and KDUMP == "winps":
                    E("act", lambda a, bank=bank: a.activation(
                        out=tr[:, 0:TT], in_=bank, func=AF.Copy),
                      reads=[bank], writes=[tr])
                    dump("winps", [tr[:, 0:TT]])
                if o < NCG:
                    rv = raw[:, o * (TT + 3) + 3: (o + 1) * (TT + 3)]
                    E("act", lambda a, rv=rv, bank=bank: a.activation(
                        out=rv, in_=bank, func=AF.Copy),
                      reads=[bank], writes=[raw])
                else:
                    gv = gbuf[:, (o - NCG) * TT:(o - NCG + 1) * TT]
                    E("act", lambda a, gv=gv, bank=bank: a.activation(
                        out=gv, in_=bank, func=AF.Silu),
                      reads=[bank], writes=[gbuf])
            if "conv" not in STAGES:
                return
            for cg in range(NCG):
                rb = raw[:, cg * (TT + 3): cg * (TT + 3) + 3]
                E("dve", lambda v, cg=cg, rb=rb: v.tensor_copy(
                    out=rb, in_=carry[:, cg * 3:(cg + 1) * 3]),
                  reads=[carry], writes=[raw])
            for cg in range(NCG):
                base = cg * (TT + 3)
                cur = raw[:, base + 3: base + 3 + TT]
                accv = sf[cg % 2]
                E("dve", lambda v, cg=cg, cur=cur, accv=accv: v.tensor_scalar(
                    out=accv, in0=cur, scalar1=wcv[:, cg * 4 + 3:cg * 4 + 4],
                    scalar2=bcv[:, cg:cg + 1], op0=ALU.mult, op1=ALU.add),
                  reads=[raw, wcv, bcv], writes=[accv])
                for j in range(1, 4):
                    shifted = raw[:, base + 3 - j: base + 3 - j + TT]
                    E("dve", lambda v, cg=cg, j=j, shifted=shifted, accv=accv:
                      v.scalar_tensor_tensor(
                          out=accv, in0=shifted,
                          scalar=wcv[:, cg * 4 + 3 - j:cg * 4 + 4 - j],
                          in1=accv, op0=ALU.mult, op1=ALU.add),
                      reads=[raw, wcv, accv], writes=[accv])
                E("act", lambda a, cg=cg, accv=accv: a.activation(
                    out=xs[:, cg * TT:(cg + 1) * TT], in_=accv, func=AF.Silu),
                  reads=[accv], writes=[xs])
                if cg == NCG - 1:
                    dump("xs", [xs])
                    dump("g", [gbuf])
                E("dve", lambda v, cg=cg: v.tensor_copy(
                    out=carry[:, cg * 3:(cg + 1) * 3],
                    in_=raw[:, cg * (TT + 3) + TT: cg * (TT + 3) + TT + 3]),
                  reads=[raw], writes=[carry])
            if "bx" not in STAGES:
                pass
            elif True:
                emit_bx(tile)
            if "dt" not in STAGES:
                return
            emit_cg_loop(par, tile)

        def emit_bx(tile):
            wxslab = wdtslab[0]
            E("qw0", lambda s, wxslab=wxslab: s.dma_start(
                out=wxslab, in_=wxs),
              reads=[wxs.tensor.name], writes=[wxslab])
            mms = []
            for k in range(NCG):
                mms.append(lambda t, k=k, wxslab=wxslab: t.matmul(
                    out=ps[0],
                    lhsT=wxslab[:, k * 128:(k + 1) * 128],
                    rhs=xs[:, k * TT:(k + 1) * TT],
                    start=(k == 0), stop=(k == NCG - 1)))
            G("pe", mms, reads=[wxslab, xs], writes=[ps[0]])
            E("act", lambda a: a.activation(out=Bsb, in_=ps[0][0:N_STATE, :],
                                            func=AF.Copy),
              reads=[ps[0]], writes=[Bsb])
            par2 = tile % 2
            E("qm", lambda s, par2=par2: s.dma_start(
                out=bstg[par2].rearrange("(n t) -> n t", n=N_STATE),
                in_=Bsb),
              reads=[Bsb], writes=[bstg.tensor.name])
            E("qm", lambda s, par2=par2: s.dma_start(
                out=brep, in_=bstg[par2].unsqueeze(0).broadcast_to(
                    [128, N_STATE * TT])),
              reads=[bstg.tensor.name], writes=[brep])
            dump("brep", [brep])
        def emit_cg_loop(par, tile):
            if os.environ.get("KDTEARLY") == "1":
                for cg in (0, 1):
                    E("qw" + str(cg % 2), lambda s, cg=cg: s.dma_start(
                        out=wdtslab[cg], in_=wdts[cg]),
                      reads=[wdts.tensor.name], writes=[wdtslab[cg]])
            for cg in range(NCG):
                slab = wdtslab[cg % 2]
                E("qw" + str(cg % 2), lambda s, cg=cg, slab=slab: s.dma_start(
                    out=slab, in_=wdts[cg]),
                  reads=[wdts.tensor.name], writes=[slab])
                bank = ps[6 + (cg % 2)]
                mms = []
                for k in range(NCG):
                    mms.append(lambda t, cg=cg, k=k, slab=slab, bank=bank:
                               t.matmul(out=bank,
                                        lhsT=slab[:, k * 128:(k + 1) * 128],
                                        rhs=xs[:, k * TT:(k + 1) * TT],
                                        start=(k == 0), stop=False))
                mms.append(lambda t, cg=cg, bank=bank: t.matmul(
                    out=bank, lhsT=bdtb[:, cg * 128:(cg + 1) * 128],
                    rhs=ones_row, start=False, stop=True))
                G("pe", mms, reads=[slab, xs, bdtb, ones_row], writes=[bank])
                if "dtsp" not in STAGES:
                    continue
                spv = sf[cg % 2]
                E("act", lambda a, spv=spv, bank=bank: a.activation(
                    out=spv, in_=bank, func=AF.Abs),
                  reads=[bank], writes=[spv])
                E("act", lambda a, spv=spv: a.activation(
                    out=spv, in_=spv, func=AF.Exp, scale=-1.0),
                  reads=[spv], writes=[spv])
                E("act", lambda a, spv=spv: a.activation(
                    out=spv, in_=spv, func=AF.Ln, bias=1.0),
                  reads=[spv], writes=[spv])
                dtv = dtb[cg % 2]
                E("dve", lambda v, dtv=dtv, bank=bank: v.tensor_single_scalar(
                    out=dtv, in_=bank, scalar=0.0, op=ALU.max),
                  reads=[bank], writes=[dtv])
                E("dve", lambda v, dtv=dtv, spv=spv: v.tensor_tensor(
                    out=dtv, in0=dtv, in1=spv, op=ALU.add),
                  reads=[dtv, spv], writes=[dtv])
                if "dtu" not in STAGES:
                    continue
                E("dve", lambda v, cg=cg, dtv=dtv: v.tensor_tensor(
                    out=ub[cg % 2], in0=dtv, in1=xs[:, cg * TT:(cg + 1) * TT],
                    op=ALU.mult),
                  reads=[dtv, xs], writes=[ub[cg % 2]])
                if "dtb" not in STAGES:
                    continue
                E("dve", lambda v, cg=cg: v.tensor_tensor(
                    out=bbig.rearrange("p (n t) -> p n t", n=N_STATE),
                    in0=ub[cg % 2].unsqueeze(1).broadcast_to([128, N_STATE, TT]),
                    in1=brep.rearrange("p (n t) -> p n t", n=N_STATE),
                    op=ALU.mult),
                  reads=[ub[cg % 2], brep], writes=[bbig])
                if "scan" not in STAGES:
                    continue
                for n in range(N_STATE):
                    dav = yb[n % 2]
                    E("act", lambda a, cg=cg, n=n, dav=dav, dtv=dtv:
                      a.activation(out=dav, in_=dtv, func=AF.Exp,
                                   scale=A4[:, cg * N_STATE + n:
                                            cg * N_STATE + n + 1]),
                      reads=[dtv, A4], writes=[dav])
                    E("dve", lambda v, cg=cg, n=n, dav=dav: v.tensor_tensor_scan(
                        out=hbuf[:, n * TT:(n + 1) * TT], data0=dav,
                        data1=bbig[:, n * TT:(n + 1) * TT],
                        initial=state[:, cg * N_STATE + n:cg * N_STATE + n + 1],
                        op0=ALU.mult, op1=ALU.add),
                      reads=[dav, bbig, state], writes=[hbuf])
                hv = hbuf.rearrange("p (n t) -> p n t", n=N_STATE)
                E("dve", lambda v, cg=cg, hv=hv: v.tensor_copy(
                    out=state[:, cg * N_STATE:(cg + 1) * N_STATE],
                    in_=hv[:, :, TT - 1]),
                  reads=[hbuf], writes=[state])
                for i in range(8):
                    E("dve", lambda v, i=i: v.tensor_tensor(
                        out=tr[:, i * TT:(i + 1) * TT],
                        in0=hbuf[:, 2 * i * TT:(2 * i + 1) * TT],
                        in1=hbuf[:, (2 * i + 1) * TT:(2 * i + 2) * TT],
                        op=ALU.add),
                      reads=[hbuf], writes=[tr])
                for i in range(4):
                    E("dve", lambda v, i=i: v.tensor_tensor(
                        out=tr[:, i * TT:(i + 1) * TT],
                        in0=tr[:, 2 * i * TT:(2 * i + 1) * TT],
                        in1=tr[:, (2 * i + 1) * TT:(2 * i + 2) * TT],
                        op=ALU.add),
                      reads=[tr], writes=[tr])
                if "gate" not in STAGES:
                    continue
                o1, o2 = yb[0], yb[1]
                E("dve", lambda v, o1=o1: v.tensor_tensor(
                    out=o1, in0=tr[:, 0:TT], in1=tr[:, TT:2 * TT], op=ALU.add),
                  reads=[tr], writes=[o1])
                E("dve", lambda v, o2=o2: v.tensor_tensor(
                    out=o2, in0=tr[:, 2 * TT:3 * TT], in1=tr[:, 3 * TT:4 * TT],
                    op=ALU.add),
                  reads=[tr], writes=[o2])
                E("dve", lambda v, cg=cg, o1=o1: v.scalar_tensor_tensor(
                    out=o1, in0=xs[:, cg * TT:(cg + 1) * TT],
                    scalar=Dcol[:, cg:cg + 1], in1=o1, op0=ALU.mult, op1=ALU.add),
                  reads=[xs, Dcol, o1], writes=[o1])
                E("dve", lambda v, o1=o1, o2=o2: v.tensor_tensor(
                    out=o1, in0=o1, in1=o2, op=ALU.add),
                  reads=[o1, o2], writes=[o1])
                zv = raw[:, cg * (TT + 3): cg * (TT + 3) + TT]
                E("dve", lambda v, cg=cg, o1=o1, zv=zv: v.tensor_tensor(
                    out=zv, in0=o1, in1=gbuf[:, cg * TT:(cg + 1) * TT],
                    op=ALU.mult),
                  reads=[o1, gbuf], writes=[raw])
                if cg == 0:
                    dump("y0", [o1])
                if cg == NCG - 1:
                    dump("z", [raw])
            if "wout" not in STAGES:
                return
            for o in range(NDC):
                slab = wdtslab[o % 2]
                E("qw" + str(o % 2), lambda s, o=o, slab=slab: s.dma_start(
                    out=slab, in_=wouts[o]),
                  reads=[wouts.tensor.name], writes=[slab])
                bank = ps[4 + (o % 2)]
                mms = []
                for k in range(NCG):
                    zk = raw[:, k * (TT + 3): k * (TT + 3) + TT]
                    mms.append(lambda t, o=o, k=k, slab=slab, bank=bank, zk=zk:
                               t.matmul(out=bank,
                                        lhsT=slab[:, k * 128:(k + 1) * 128],
                                        rhs=zk, start=(k == 0),
                                        stop=(k == NCG - 1)))
                G("pe", mms, reads=[slab, raw], writes=[bank])
                E("dve", lambda v, o=o, par=par, bank=bank: v.scalar_tensor_tensor(
                    out=xch(par, o), in0=bank, scalar=1.0, in1=xch(par, o),
                    op0=ALU.mult, op1=ALU.add),
                  reads=[bank, xbufs[par]], writes=[xbufs[par]])

        for tile in range(n_tiles):
            par = tile % 2
            t0 = tile * TT
            for dc in range(NDC):
                E("qx", lambda s, dc=dc, par=par, t0=t0: s.dma_start(
                    out=xch(par, dc),
                    in_=xT[dc * 128:(dc + 1) * 128, t0:t0 + TT]),
                  reads=[], writes=[xbufs[par]])
            if "ffn1" in STAGES:
                emit_ffn(par)
            emit_mixer(par, tile)
            if "ffn2" in STAGES:
                emit_ffn(par)
            for dc in range(NDC):
                E("qx", lambda s, dc=dc, par=par, t0=t0: s.dma_start(
                    out=yT[dc * 128:(dc + 1) * 128, t0:t0 + TT],
                    in_=xch(par, dc)),
                  reads=[xbufs[par]], writes=[])

        total_qx = bd.count["qx"]
        total_qm = bd.count["qm"]
        bd.prog["sync"].append(
            lambda s, total_qx=total_qx: s.wait_ge(bd.sems["qx"], total_qx))
        bd.prog["sync"].append(
            lambda s, total_qm=total_qm: s.wait_ge(bd.sems["qm"], total_qm))

        with nc.Block() as block:
            if bd.prog["sync"]:
                block.sync(lambda e: [f(e) for f in bd.prog["sync"]])
            if bd.prog["pe"]:
                block.tensor(lambda e: [f(e) for f in bd.prog["pe"]])
            if bd.prog["act"]:
                block.scalar(lambda e: [f(e) for f in bd.prog["act"]])
            if bd.prog["dve"]:
                block.vector(lambda e: [f(e) for f in bd.prog["dve"]])
            if bd.prog["pool"]:
                block.gpsimd(lambda e: [f(e) for f in bd.prog["pool"]])

    return nc



def kernel(L0_attn_x, L0_attn_W_in, L0_attn_W_conv, L0_attn_b_conv, L0_attn_W_x,
           L0_attn_W_dt, L0_attn_b_dt, L0_attn_W_out, L0_attn_A_log, L0_attn_D,
           L0_ffn_W1, L0_ffn_b1, L0_ffn_quantum_phases,
           L0_ffn_quantum_weights_real, L0_ffn_quantum_weights_imag,
           L0_ffn_W2, L0_ffn_b2):
    B, S, D = L0_attn_x.shape
    assert (B, S, D) == (4, 8192, 1024)
    n_tiles = S // TT

    base = host_base(L0_ffn_W1, L0_ffn_b1, L0_ffn_quantum_phases,
                     L0_ffn_quantum_weights_real, L0_ffn_quantum_weights_imag,
                     L0_ffn_W2, L0_ffn_b2, L0_attn_W_in, L0_attn_W_conv,
                     L0_attn_b_conv, L0_attn_W_x, L0_attn_W_dt, L0_attn_b_dt,
                     L0_attn_W_out, L0_attn_A_log, L0_attn_D)

    xs_np = np.asarray(L0_attn_x, np.float32)
    in_maps = []
    for r in range(N_CORES):
        m = dict(base)
        m["xT"] = np.ascontiguousarray(xs_np[r % 4].T)
        in_maps.append(m)

    nc = build_program(n_tiles)
    res = run_bass_kernel_spmd(nc, in_maps, core_ids=list(range(N_CORES)))
    out = np.empty((4, S, D), np.float32)
    for b in range(4):
        out[b] = np.asarray(res.results[b]["yT"]).T
    return out


def host_base(L0_ffn_W1, L0_ffn_b1, L0_ffn_quantum_phases,
              L0_ffn_quantum_weights_real, L0_ffn_quantum_weights_imag,
              L0_ffn_W2, L0_ffn_b2, L0_attn_W_in, L0_attn_W_conv,
              L0_attn_b_conv, L0_attn_W_x, L0_attn_W_dt, L0_attn_b_dt,
              L0_attn_W_out, L0_attn_A_log, L0_attn_D):
    f32 = np.float32
    cc = np.ascontiguousarray

    def row(v):
        return cc(np.asarray(v, f32).reshape(1, -1))

    def colmaj(v, w):
        return cc(np.asarray(v, f32).reshape(w, 128).T)

    def slabs(WkT):
        K, O = WkT.shape
        nkc, noc = K // 128, O // 128
        v = WkT.reshape(nkc, 128, noc, 128)
        return cc(v.transpose(2, 1, 0, 3).reshape(noc, 128, nkc * 128))

    base = {
        "w1sl": slabs(np.asarray(L0_ffn_W1, f32)),
        "w2sl": slabs(np.asarray(L0_ffn_W2, f32)),
        "winsl": slabs(np.asarray(L0_attn_W_in, f32).T),
        "wdtsl": slabs(np.asarray(L0_attn_W_dt, f32).T),
        "woutsl": slabs(np.asarray(L0_attn_W_out, f32).T),
        "wxsl": cc(np.tile(np.asarray(L0_attn_W_x, f32)[16:32, :].T.reshape(
            NCG, 128, N_STATE).transpose(1, 0, 2), (1, 1, 8)).reshape(
            128, NCG * 128)),
        "b1r": row(L0_ffn_b1),
        "b2r": row(L0_ffn_b2),
        "bdtr": row(L0_attn_b_dt),
        "thc": colmaj(L0_ffn_quantum_phases, NFC),
        "wrc": colmaj(L0_ffn_quantum_weights_real, NFC),
        "wic": colmaj(L0_ffn_quantum_weights_imag, NFC),
        "Alog": cc(np.asarray(L0_attn_A_log, f32)),
        "Dc": colmaj(L0_attn_D, NCG),
        "bconvc": colmaj(L0_attn_b_conv, NCG),
        "wconvc": cc(np.asarray(L0_attn_W_conv, f32).reshape(D_INNER, 4)),
    }
    return base
